# version 03
# baseline (speedup 1.0000x reference)
import numpy as np
import ml_dtypes
import sys

for p in ("/opt/trn_rl_repo", "/root/.axon_site/_ro/trn_rl_repo"):
    if p not in sys.path:
        sys.path.insert(0, p)
import concourse.bass as bass
import concourse.mybir as mybir
from concourse.bass_utils import run_bass_kernel_spmd

F32 = mybir.dt.float32
BF16 = mybir.dt.bfloat16
AF = mybir.ActivationFunctionType
ALU = mybir.AluOpType

S = 4096
H = 1024
DFF = 4096
NEXP = 8
EPS = 1e-5


def build():
    nc = bass.Bass("TRN2", target_bir_lowering=False)
    d_x = nc.dram_tensor("x", [S, H], F32, kind="ExternalInput")
    d_wqkv = nc.dram_tensor("w_qkv", [H, 3 * H], F32, kind="ExternalInput")
    d_bqkv = nc.dram_tensor("b_qkv", [3 * H], F32, kind="ExternalInput")
    d_wout = nc.dram_tensor("w_out", [H, H], F32, kind="ExternalInput")
    d_bout = nc.dram_tensor("b_out", [H], F32, kind="ExternalInput")
    d_w1 = nc.dram_tensor("w1", [NEXP, DFF, H], F32, kind="ExternalInput")
    d_b1 = nc.dram_tensor("b1", [NEXP, DFF], F32, kind="ExternalInput")
    d_w2 = nc.dram_tensor("w2", [NEXP, H, DFF], F32, kind="ExternalInput")
    d_b2 = nc.dram_tensor("b2", [NEXP, H], F32, kind="ExternalInput")
    d_x2 = nc.dram_tensor("x2s", [S, H], F32, kind="Internal")
    d_ctx = nc.dram_tensor("ctxs", [128, 8 * 4096], BF16, kind="Internal")
    d_out = nc.dram_tensor("out", [S, H], F32, kind="ExternalOutput")

    from contextlib import ExitStack
    with ExitStack() as es:
        pass_holder = None
        def sb(name, shape, dt):
            return es.enter_context(nc.sbuf_tensor(name, shape, dt))

        def psum(name, shape, dt=F32):
            return es.enter_context(nc.psum_tensor(name, shape, dt))

        identB = sb("identB", [128, 128], BF16)
        identF = sb("identF", [128, 128], F32)
        msk = [sb(f"msk{r}", [128, 512], BF16) for r in range(4)]
        onesP = sb("onesP", [65, 64], F32)
        btld = sb("btld", [128, 384], F32)
        biasT = sb("biasT", [128, 352], F32)
        xbuf = [sb(f"xbuf{i}", [128, H], F32) for i in range(2)]
        xln_t = [sb(f"xln_t{i}", [128, H], BF16) for i in range(2)]
        mv = [sb(f"mv{i}", [128, 8], F32) for i in range(4)]
        st12 = [sb(f"st12{i}", [128, 12], F32) for i in range(4)]
        epsv = sb("epsv", [128, 1], F32)
        b_es = ExitStack()
        b_es.__enter__()

        def sbB(name, shape, dt):
            return b_es.enter_context(nc.sbuf_tensor(name, shape, dt))
        wq_bf = sbB("wq_bf", [128, 8 * 3 * H], BF16)
        xlnT = sbB("xlnT", [128, 8 * S], BF16)
        QQ = sbB("QQ", [128, S], BF16)
        KK = sbB("KK", [128, S], BF16)
        VVT = sbB("VVT", [128, S], BF16)
        Vx = [sbB(f"Vx{h}", [128, 32 * 65], BF16) for h in range(2)]
        P = [sbB(f"P{i}", [128, 1024], BF16) for i in range(3)]
        cta = [sbB(f"cta{i}", [65, 1024], F32) for i in range(2)]
        rr = sbB("rr", [65, 1024], F32)
        ctxp = [sbB(f"ctxp{i}", [128, 4096], BF16) for i in range(2)]

        SS = [psum(f"SS{i}", [128, 1024]) for i in range(2)]
        CX = [psum(f"CX{i}", [128, 1024]) for i in range(2)]
        SSb = [s.bitcast(BF16) for s in SS]
        CXb = [c.bitcast(BF16) for c in CX]
        HB = [SS[0][:, 0:512], SS[0][:, 512:1024],
              SS[1][:, 0:512], SS[1][:, 512:1024]]

        class Cat:
            def __init__(self, name):
                self.sem = es.enter_context(nc.semaphore(name))
                self.n = 0

            def bump(self, step=1):
                self.n += step
                return self.n
        names = ["d_in", "d_cast", "d_st",
                 "p_g", "s_sc", "s_av", "s_bc",
                 "a_sd", "a_g", "s_exp",
                 "v_mv", "v_ap", "v_ev", "s_msk", "s_cp", "s_mul", "s_rcp",
                 "v_bn"]
        C = {n: Cat(n) for n in names}

        def W(eng, cat, val):
            eng.wait_ge(C[cat].sem, val)

        nb0 = C["d_in"].bump(16 * 6)
        wq_dma = C["d_cast"].bump(16)
        with nc.Block() as block:
            @block.sync
            def _(sync):
                b1r = d_b1.ap().rearrange("e (r p) -> (e r) p", p=128)
                for dst, src in [
                    (btld[0:24, 0:128],
                     d_bqkv.ap().rearrange("(r p) -> r p", p=128)),
                    (btld[24:32, 0:128],
                     d_bout.ap().rearrange("(r p) -> r p", p=128)),
                    (btld[32:96, 0:128],
                     d_b2.ap().rearrange("e (r p) -> (e r) p", p=128)),
                    (btld[96:128, 0:128], b1r[0:32]),
                    (btld[0:128, 128:256], b1r[32:160]),
                    (btld[0:96, 256:384], b1r[160:256]),
                ]:
                    sync.dma_start(dst, src).then_inc(C["d_in"].sem, 16)

            @block.gpsimd
            def _(g):
                for ident in (identB, identF):
                    g.memset(ident.ap(), 0.0)
                    g.affine_select(out=ident.ap(), in_=ident.ap(),
                                    compare_op=ALU.not_equal, fill=1.0,
                                    base=0, pattern=[[-1, 128]],
                                    channel_multiplier=1)
                g.memset(onesP.ap(), 1.0)
                g.memset(epsv.ap(), EPS)
                for r in range(4):
                    g.memset(msk[r].ap(), 1.0)
                    g.affine_select(out=msk[r].ap(), in_=msk[r].ap(),
                                    compare_op=ALU.is_ge, fill=0.0,
                                    base=128 * r, pattern=[[-1, 512]],
                                    channel_multiplier=1)
                for h in range(2):
                    g.memset(Vx[h].ap().rearrange(
                        "p (b s) -> p b s", s=65)[:, :, 64:65], 1.0)
                g.dma_start(
                    wq_bf.ap().rearrange("p (c f) -> p c f", f=3 * H),
                    d_wqkv.ap().rearrange("(c p) f -> p c f", p=128)
                ).then_inc(C["d_cast"].sem, 16)

        bias_tr = C["p_g"].bump()
        bias_ev = C["v_ev"].bump()
        with nc.Block() as block:
            @block.tensor
            def _(t):
                W(t, "d_in", nb0)
                t.transpose(CX[0][:, 0:128], btld[0:128, 0:128], identF.ap())
                t.transpose(CX[0][:, 128:256], btld[0:128, 128:256],
                            identF.ap())
                t.transpose(CX[0][:, 256:352], btld[0:96, 256:384],
                            identF[0:96, 0:96]).then_inc(C["p_g"].sem, 1)

            @block.vector
            def _(v):
                W(v, "p_g", bias_tr)
                v.tensor_copy(biasT.ap(), CX[0][:, 0:352])
                v.tensor_scalar_mul(biasT[:, 0:8], biasT[:, 0:8],
                                    0.125).then_inc(C["v_ev"].sem, 1)

        def ln_stats(tile_ap, par, dve_ops, act_ops, pre_waits=()):
            mvp, stp = mv[par], st12[par]
            mv_n = C["v_mv"].bump()
            bn1 = C["v_bn"].bump()

            def f_stats(v, pw=tuple(pre_waits)):
                for cn, vv in pw:
                    W(v, cn, vv)
                v.bn_stats(stp[:, 0:6], tile_ap[:, 0:512])
                v.bn_stats(stp[:, 6:12],
                           tile_ap[:, 512:1024]).then_inc(C["v_bn"].sem, 1)
                W(v, "v_bn", bn1)
                v.bn_aggr(mvp[:, 0:2], stp.ap()).then_inc(C["v_mv"].sem, 1)
            dve_ops.append(f_stats)
            sd_n = C["a_sd"].bump()

            def f_sqrt(s):
                W(s, "v_mv", mv_n)
                s.activation(mvp[:, 2:3], mvp[:, 1:2], AF.Sqrt,
                             bias=epsv.ap()).then_inc(C["a_sd"].sem, 1)
            act_ops.append(f_sqrt)
            return sd_n

        def ln_apply(tile_ap, out_ap, par, sd_n, dve_ops, pre_waits=()):
            mvp = mv[par]
            ap_n = C["v_ap"].bump()
            bn2 = C["v_bn"].bump()

            def f_apply(v, pw=tuple(pre_waits)):
                for cn, vv in pw:
                    W(v, cn, vv)
                W(v, "a_sd", sd_n)
                v.reciprocal(mvp[:, 3:4],
                             mvp[:, 2:3]).then_inc(C["v_bn"].sem, 1)
                W(v, "v_bn", bn2)
                v.tensor_scalar(out_ap, tile_ap, mvp[:, 0:1], mvp[:, 3:4],
                                ALU.subtract, ALU.mult).then_inc(
                    C["v_ap"].sem, 1)
            dve_ops.append(f_apply)
            return ap_n

        sync_ops, dve_ops, act_ops, pe_ops = [], [], [], []
        a_ap, a_tr, a_ev, a_sd = {}, {}, {}, {}

        def a_emit_apply(ti):
            par = ti % 2
            a_ap[ti] = ln_apply(xbuf[par].ap(), xln_t[par].ap(), par,
                                a_sd[ti], dve_ops)

        for ti in range(32):
            par = ti % 2
            need_ap = a_ap.get(ti - 2)

            def mk_load(ti=ti, par=par, need_ap=need_ap):
                def f(sync):
                    if need_ap is not None:
                        W(sync, "v_ap", need_ap)
                    sync.dma_start(
                        xbuf[par].ap(),
                        d_x[128 * ti:128 * (ti + 1), :]).then_inc(
                        C["d_in"].sem, 16)
                return f
            sync_ops.append(mk_load())
            xd = C["d_in"].bump(16)

            pre = [("d_in", xd)]
            if ti >= 2:
                pre.append(("p_g", a_tr[ti - 2]))
            a_sd[ti] = ln_stats(xbuf[par].ap(), par, dve_ops, act_ops,
                                pre_waits=pre)
            if ti >= 1:
                a_emit_apply(ti - 1)

            if ti >= 1:
                tj = ti - 1
                need_ev = a_ev.get(tj - 2)
                a_tr[tj] = C["p_g"].bump()

                def mk_tr(tj=tj, need_ev=need_ev, ap_n=a_ap[tj]):
                    bank = SSb[tj % 2][:, 0:1024]
                    def f(t):
                        W(t, "v_ap", ap_n)
                        if need_ev is not None:
                            W(t, "v_ev", need_ev)
                        for c in range(7):
                            t.transpose(
                                bank[:, 128 * c:128 * (c + 1)],
                                xln_t[tj % 2][:, 128 * c:128 * (c + 1)],
                                identB.ap())
                        t.transpose(bank[:, 896:1024],
                                    xln_t[tj % 2][:, 896:1024],
                                    identB.ap()).then_inc(C["p_g"].sem, 1)
                    return f
                pe_ops.append(mk_tr())

                a_ev[tj] = C["v_ev"].bump()

                def mk_ev(tj=tj, tr_n=a_tr[tj]):
                    bank = SSb[tj % 2][:, 0:1024]
                    dst = xlnT.ap().rearrange(
                        "p (c s) -> p c s",
                        s=S)[:, :, 128 * tj:128 * (tj + 1)]
                    src = bank.rearrange("p (c s) -> p c s", s=128)
                    def f(v):
                        W(v, "p_g", tr_n)
                        v.tensor_copy(dst, src).then_inc(C["v_ev"].sem, 1)
                    return f
                dve_ops.append(mk_ev())

        a_emit_apply(31)
        tj = 31
        need_ev = a_ev.get(tj - 2)
        a_tr[tj] = C["p_g"].bump()

        def mk_tr_last(tj=tj, need_ev=need_ev, ap_n=a_ap[tj]):
            bank = SSb[tj % 2][:, 0:1024]
            def f(t):
                W(t, "v_ap", ap_n)
                if need_ev is not None:
                    W(t, "v_ev", need_ev)
                for c in range(7):
                    t.transpose(bank[:, 128 * c:128 * (c + 1)],
                                xln_t[tj % 2][:, 128 * c:128 * (c + 1)],
                                identB.ap())
                t.transpose(bank[:, 896:1024], xln_t[tj % 2][:, 896:1024],
                            identB.ap()).then_inc(C["p_g"].sem, 1)
            return f
        pe_ops.append(mk_tr_last())
        a_ev[tj] = C["v_ev"].bump()

        def mk_ev_last(tj=tj, tr_n=a_tr[tj]):
            bank = SSb[tj % 2][:, 0:1024]
            dst = xlnT.ap().rearrange(
                "p (c s) -> p c s", s=S)[:, :, 128 * tj:128 * (tj + 1)]
            src = bank.rearrange("p (c s) -> p c s", s=128)
            def f(v):
                W(v, "p_g", tr_n)
                v.tensor_copy(dst, src).then_inc(C["v_ev"].sem, 1)
            return f
        dve_ops.append(mk_ev_last())

        with nc.Block() as block:
            @block.sync
            def _(sync):
                for f in sync_ops:
                    f(sync)

            @block.vector
            def _(v):
                for f in dve_ops:
                    f(v)

            @block.scalar
            def _(s):
                for f in act_ops:
                    f(s)

            @block.tensor
            def _(t):
                for f in pe_ops:
                    f(t)

        units = []
        for g in range(8):
            for qt in range(8):
                kbs = list(range(4 * qt, 32))
                for ki, kb in enumerate(kbs):
                    units.append((g, 8 * g + qt, qt, ki, len(kbs), kb))
        first_u, last_u = {}, {}
        for u, (g, gqt, qt, ki, nk, kb) in enumerate(units):
            first_u.setdefault(gqt, u)
            last_u[gqt] = u

        pe_ops, act_ops, dve_ops, bsync_ops = [], [], [], []
        sc_fns, av_fns = {}, {}
        qkv_pe = {g: [] for g in range(8)}
        exp_h, av_h, sc_h = {}, {}, {}
        cp_h, mul_h, rcp_h = {}, {}, {}
        qe_done, ctxdma_h = {}, {}
        pend, pend2 = [], []
        bc_fns = {}

        for g in range(8):
            exp_before = C["s_exp"].n
            qe_base = C["v_ev"].n
            for j in range(24):
                kind, qt = j // 8, j % 8
                col0 = kind * H + 128 * g
                dstq = [QQ, KK, VVT][kind]
                bias_col = [g, 8 + g, 16 + g][kind]
                hb = HB[j % 4]
                need_qe = (qe_base + (j - 4) + 1) if j >= 4 else None
                mm_n = C["p_g"].bump()

                def mk_grp(j=j, qt=qt, col0=col0, hb=hb, g=g,
                           exp_before=exp_before, need_qe=need_qe):
                    def f(t):
                        if j == 0 and g > 0:
                            W(t, "s_exp", exp_before)
                        if need_qe is not None:
                            W(t, "v_ev", need_qe)
                        for c in range(8):
                            i = t.matmul(
                                hb,
                                wq_bf[:, 3 * H * c + col0:
                                      3 * H * c + col0 + 128],
                                xlnT[:, S * c + 512 * qt:
                                     S * c + 512 * qt + 512],
                                start=(c == 0), stop=(c == 7))
                        i.then_inc(C["p_g"].sem, 1)
                    return f
                qkv_pe[g].append(mk_grp())

                C["v_ev"].bump()

                def mk_qev(qt=qt, dstq=dstq, bias_col=bias_col, hb=hb,
                           mm_n=mm_n, kind=kind):
                    def f(v):
                        W(v, "p_g", mm_n)
                        dst = dstq[:, 512 * qt:512 * (qt + 1)]
                        if kind == 0:
                            v.tensor_scalar(
                                dst, hb, 0.125,
                                biasT[:, bias_col:bias_col + 1],
                                ALU.mult, ALU.add).then_inc(C["v_ev"].sem, 1)
                        else:
                            v.tensor_scalar_add(
                                dst, hb,
                                biasT[:, bias_col:bias_col + 1]).then_inc(
                                C["v_ev"].sem, 1)
                    return f
                dve_ops.append(mk_qev())

            vev_h = {}
            for b in range(4):
                h, half = b // 2, b % 2
                bank = CXb[0][:, 1024 * (b % 2):1024 * (b % 2 + 1)]
                vv_need = qe_base + 20 + 4 * half
                tr_n = C["p_g"].bump()

                def mk_vtr(b=b, h=h, half=half, bank=bank, vv_need=vv_need,
                           g=g, bank_free=vev_h.get(b - 2)):
                    def f(t):
                        if b == 0 and g > 0:
                            W(t, "s_mul", 8 * g - 1)
                        if bank_free is not None:
                            W(t, "v_ev", bank_free)
                        W(t, "v_ev", vv_need)
                        for i2 in range(16):
                            kb = 16 * half + i2
                            ii = t.transpose(
                                bank[:, 64 * i2:64 * (i2 + 1)],
                                VVT[64 * h:64 * h + 64,
                                    128 * kb:128 * (kb + 1)],
                                identB[64 * h:64 * h + 64,
                                       64 * h:64 * h + 64],
                                tile_position=(64 * h, 0))
                        ii.then_inc(C["p_g"].sem, 1)
                    return f
                qkv_pe[g].append(mk_vtr())

                vev_h[b] = C["v_ev"].bump()

                def mk_vev(h=h, half=half, bank=bank, tr_n=tr_n):
                    dst = Vx[h].ap().rearrange(
                        "p (k s) -> p k s", s=65)[:, 16 * half:
                                                  16 * half + 16, 0:64]
                    src = bank.rearrange("p (k s) -> p k s", s=64)
                    def f(v):
                        W(v, "p_g", tr_n)
                        v.tensor_copy(dst, src).then_inc(C["v_ev"].sem, 1)
                    return f
                dve_ops.append(mk_vev())
            qe_done[g] = C["v_ev"].n

            lu0 = first_u[8 * g]
            for u in range(lu0, lu0 + 144):
                g_, gqt, qt, ki, nk, kb = units[u]
                q0 = 512 * qt
                sbuf = u % 2
                pbuf = u % 3
                par = gqt % 2
                diag = ki < 4
                first = ki == 0
                last = ki == nk - 1

                waits = []
                if u >= 2:
                    waits.append(("s_exp", exp_h[u - 2]))
                if u == lu0:
                    waits.append(("v_ev", qe_done[g]))
                sc_h[u] = C["s_sc"].bump()

                def mk_sc(q0=q0, kb=kb, sbuf=sbuf, waits=tuple(waits)):
                    def f(t):
                        for cn, vv in waits:
                            W(t, cn, vv)
                        t.matmul(SS[sbuf][:, 0:512],
                                 KK[0:64, 128 * kb:128 * kb + 128],
                                 QQ[0:64, q0:q0 + 512],
                                 start=True, stop=True, tile_position=(0, 0))
                        t.matmul(SS[sbuf][:, 512:1024],
                                 KK[64:128, 128 * kb:128 * kb + 128],
                                 QQ[64:128, q0:q0 + 512],
                                 start=True, stop=True,
                                 tile_position=(64, 0)).then_inc(
                            C["s_sc"].sem, 1)
                    return f
                sc_fns[u] = mk_sc()

                waits = [("s_sc", sc_h[u])]
                if u >= 3:
                    waits.append(("s_av", av_h[u - 3]))
                exp_h[u] = C["s_exp"].bump()

                def mk_exp(sbuf=sbuf, pbuf=pbuf, waits=tuple(waits)):
                    def f(s):
                        for cn, vv in waits:
                            W(s, cn, vv)
                        s.activation(P[pbuf].ap(), SS[sbuf].ap(),
                                     AF.Exp).then_inc(C["s_exp"].sem, 1)
                    return f
                act_ops.append(mk_exp())

                if diag:
                    msk_n = C["s_msk"].bump()

                    def mk_msk(pbuf=pbuf, r=ki, need=exp_h[u]):
                        def f(v):
                            W(v, "s_exp", need)
                            v.tensor_mul(P[pbuf][:, 0:512],
                                         P[pbuf][:, 0:512], msk[r].ap())
                            v.tensor_mul(P[pbuf][:, 512:1024],
                                         P[pbuf][:, 512:1024],
                                         msk[r].ap()).then_inc(
                                C["s_msk"].sem, 1)
                        return f
                    dve_ops.append(mk_msk())
                    av_wait = ("s_msk", msk_n)
                else:
                    av_wait = ("s_exp", exp_h[u])

                if ((ki == 4) or (nk == 4 and ki == 3)) and pend2:
                    dve_ops.extend(pend2)
                    pend2 = []

                waits = [av_wait]
                if first and gqt >= 2:
                    waits.append(("s_cp", cp_h[gqt - 2]))
                    waits.append(("s_mul", mul_h[gqt - 2]))
                av_h[u] = C["s_av"].bump()

                def mk_av(pbuf=pbuf, kb=kb, par=par, first=first, last=last,
                          waits=tuple(waits)):
                    def f(t):
                        for cn, vv in waits:
                            W(t, cn, vv)
                        t.matmul(CX[par][0:65, 0:512],
                                 Vx[0][:, 65 * kb:65 * kb + 65],
                                 P[pbuf][:, 0:512], start=first, stop=last)
                        t.matmul(CX[par][0:65, 512:1024],
                                 Vx[1][:, 65 * kb:65 * kb + 65],
                                 P[pbuf][:, 512:1024], start=first,
                                 stop=last).then_inc(C["s_av"].sem, 1)
                    return f
                av_fns[u] = mk_av()

                if last:
                    cp_h[gqt] = C["s_cp"].bump()

                    def mk_copy(par=par, need=av_h[u]):
                        def f(v):
                            W(v, "s_av", need)
                            v.tensor_copy(cta[par].ap(),
                                          CX[par][0:65, :]).then_inc(
                                C["s_cp"].sem, 1)
                        return f
                    dve_ops.append(mk_copy())

                    rcp_h[gqt] = C["s_rcp"].bump()

                    def mk_rcp(par=par):
                        def f(v):
                            v.reciprocal(rr[64:65, :],
                                         cta[par][64:65, :]).then_inc(
                                C["s_rcp"].sem, 1)
                        return f
                    pend.append(mk_rcp())

                    bc_n = C["s_bc"].bump()

                    def mk_bc_a(par=par, need=rcp_h[gqt]):
                        def f(t):
                            W(t, "s_rcp", need)
                            t.matmul(CX[par][0:64, 0:512], onesP[64:65, :],
                                     rr[64:65, 0:512], start=True,
                                     stop=True, tile_position=(64, 0))
                        return f

                    def mk_bc_b(par=par):
                        def f(t):
                            t.matmul(CX[par][0:64, 512:1024],
                                     onesP[64:65, :], rr[64:65, 512:1024],
                                     start=True, stop=True,
                                     tile_position=(64, 0)).then_inc(
                                C["s_bc"].sem, 1)
                        return f
                    bc_fns[gqt] = (mk_bc_a(), mk_bc_b())

                    mul_h[gqt] = C["s_mul"].bump()
                    ctx_free = ctxdma_h.get(g - 2)

                    def mk_mul(par=par, q0=q0, g=g, need=bc_n,
                               ctx_free=ctx_free):
                        def f(v):
                            W(v, "s_bc", need)
                            if ctx_free is not None:
                                W(v, "d_st", ctx_free)
                            v.tensor_mul(ctxp[g % 2][0:64, q0:q0 + 512],
                                         cta[par][0:64, 0:512],
                                         CX[par][0:64, 0:512])
                            v.tensor_mul(ctxp[g % 2][64:128, q0:q0 + 512],
                                         cta[par][0:64, 512:1024],
                                         CX[par][0:64, 512:1024]).then_inc(
                                C["s_mul"].sem, 1)
                        return f
                    pend.append(mk_mul())
                    pend2.extend(pend)
                    pend = []

            mul_need = C["s_mul"].n
            ctxdma_h[g] = C["d_st"].bump(16)

            def mk_ctxw(g=g, mul_need=mul_need):
                def f(sync):
                    W(sync, "s_mul", mul_need)
                    sync.dma_start(d_ctx[:, 4096 * g:4096 * (g + 1)],
                                   ctxp[g % 2].ap()).then_inc(
                        C["d_st"].sem, 16)
                return f
            bsync_ops.append(mk_ctxw())

        dve_ops.extend(pend2)
        pend2 = []

        place_at = {}
        for gqt, ul in last_u.items():
            if gqt + 2 in first_u:
                pos = min(ul + 18, first_u[gqt + 2] - 2)
                while pos in place_at:
                    pos -= 1
                place_at[pos] = gqt

        placed = set()
        for g in range(8):
            lu0 = first_u[8 * g]
            hi = lu0 + 144
            pe_ops.extend(qkv_pe[g])
            pe_ops.append(sc_fns[lu0])
            pe_ops.append(sc_fns[lu0 + 1])
            pe_ops.append(sc_fns[lu0 + 2])
            for u in range(lu0, hi):
                pe_ops.append(av_fns[u])
                if u + 3 < hi:
                    pe_ops.append(sc_fns[u + 3])
                if u in place_at:
                    pe_ops.append(bc_fns[place_at[u]][0])
                    placed.add(place_at[u])
                if (u - 1) in place_at:
                    pe_ops.append(bc_fns[place_at[u - 1]][1])
        for gqt in sorted(bc_fns):
            if gqt not in placed:
                pe_ops.extend(bc_fns[gqt])

        with nc.Block() as block:
            @block.tensor
            def _(t):
                W(t, "d_cast", wq_dma)
                for f in pe_ops:
                    f(t)

            @block.scalar
            def _(s):
                for f in act_ops:
                    f(s)

            @block.vector
            def _(v):
                for f in dve_ops:
                    f(v)

            @block.sync
            def _(sync):
                for f in bsync_ops:
                    f(sync)

        b_es.close()
        stage8 = [sb(f"stage8{i}", [128, 4096], BF16) for i in range(2)]
        ptok = [sb(f"ptok{i}", [128, 1024], F32) for i in range(2)]
        ln_o = [sb(f"ln_o{i}", [128, H], BF16) for i in range(2)]
        x2t = [sb(f"x2t{i}", [128, H], F32) for i in range(2)]
        outt = [sb(f"outt{i}", [128, H], F32) for i in range(2)]
        xln2T = sb("xln2T", [128, 8 * S], BF16)
        c_es = ExitStack()
        c_es.__enter__()
        wout_bf = c_es.enter_context(
            nc.sbuf_tensor("wout_bf", [128, 8 * H], BF16))
        ctxs = [c_es.enter_context(
            nc.sbuf_tensor(f"ctxs{i}", [128, 8 * 512], BF16))
            for i in range(2)]

        pe_ops, act_ops, dve_ops, sync_ops, gp_ops = [], [], [], [], []
        ctx_all = ctxdma_h[7]
        wout_dma = C["d_cast"].bump(16)

        def mk_woutload():
            def f(g):
                g.dma_start(
                    wout_bf.ap().rearrange("p (c f) -> p c f", f=H),
                    d_wout.ap().rearrange("(c p) f -> p c f", p=128)
                ).then_inc(C["d_cast"].sem, 16)
            return f
        gp_ops.append(mk_woutload())

        hb_ev = {}
        cmm_last = {}
        c_tr3, c_sd2, c_sd3, c_ap2, c_ap3 = {}, {}, {}, {}, {}
        tile_cp = {}
        tile_e3 = {}
        tr_last = {}
        c_add, c_x2w = {}, {}
        c_xd = {}
        stage_done_q = {}

        def c_stage1(ti):
            par = ti % 2
            c_ap2[ti] = ln_apply(ptok[par].ap(), ln_o[par].ap(), par,
                                 c_sd2[ti], dve_ops)
            need_x2w = c_x2w.get(ti - 2)
            c_add[ti] = C["v_ap"].bump()

            def mk_add(par=par, xd=c_xd[ti], need_x2w=need_x2w):
                def f(v):
                    W(v, "d_in", xd)
                    if need_x2w is not None:
                        W(v, "d_st", need_x2w)
                    v.tensor_add(x2t[par].ap(), xbuf[par].ap(),
                                 ln_o[par].ap()).then_inc(C["v_ap"].sem, 1)
                return f
            dve_ops.append(mk_add())

            c_x2w[ti] = C["d_st"].bump(16)

            def mk_x2w(ti=ti, par=par, add_n=c_add[ti]):
                def f(sync):
                    W(sync, "v_ap", add_n)
                    sync.dma_start(d_x2[128 * ti:128 * (ti + 1), :],
                                   x2t[par].ap()).then_inc(C["d_st"].sem, 16)
                return f
            sync_ops.append(mk_x2w())

            c_sd3[ti] = ln_stats(x2t[par].ap(), 2 + par, dve_ops, act_ops)

        def c_stage2(ti):
            par = ti % 2
            pre = []
            if (ti - 2) in c_tr3:
                pre.append(("p_g", c_tr3[ti - 2]))
            c_ap3[ti] = ln_apply(x2t[par].ap(), xln_t[par].ap(), 2 + par,
                                 c_sd3[ti], dve_ops, pre_waits=pre)
            need_e3 = tile_e3.get(ti - 2)
            tr3 = C["p_g"].bump()
            c_tr3[ti] = tr3
            bank2 = CXb[ti % 2][:, 1024:2048]

            def mk_tr3(bank2=bank2, ap3=c_ap3[ti], need_e3=need_e3,
                       par=par):
                def f(t):
                    W(t, "v_ap", ap3)
                    if need_e3 is not None:
                        W(t, "v_ev", need_e3)
                    for c in range(7):
                        t.transpose(bank2[:, 128 * c:128 * (c + 1)],
                                    xln_t[par][:, 128 * c:128 * (c + 1)],
                                    identB.ap())
                    t.transpose(bank2[:, 896:1024],
                                xln_t[par][:, 896:1024],
                                identB.ap()).then_inc(C["p_g"].sem, 1)
                return f
            pe_ops.append(mk_tr3())
            tile_e3[ti] = C["v_ev"].bump()

            def mk_ev3(ti=ti, bank2=bank2, tr3=tr3):
                dst = xln2T.ap().rearrange(
                    "p (c s) -> p c s", s=S)[:, :, 128 * ti:128 * (ti + 1)]
                src = bank2.rearrange("p (c s) -> p c s", s=128)
                def f(v):
                    W(v, "p_g", tr3)
                    v.tensor_copy(dst, src).then_inc(C["v_ev"].sem, 1)
                return f
            dve_ops.append(mk_ev3())

        for qt in range(8):
            sq = qt % 2

            def mk_ctxl(qt=qt):
                def f(sync):
                    W(sync, "d_st", ctx_all)
                    if qt >= 2:
                        W(sync, "p_g", cmm_last[qt - 2])
                    src2 = d_ctx.ap().rearrange(
                        "p (g s) -> p g s", s=4096)[:, :, 512 * qt:
                                                    512 * (qt + 1)]
                    dst = ctxs[qt % 2].ap().rearrange(
                        "p (g s) -> p g s", s=512)
                    sync.dma_start(dst, src2).then_inc(C["d_in"].sem, 16)
                return f
            sync_ops.append(mk_ctxl())
            ctx_d = C["d_in"].bump(16)

            for fo in range(8):
                k = 8 * qt + fo
                hb = HB[k % 4]
                need_ev = hb_ev.get(k - 4)
                need_stage = tr_last.get(qt - 2) if fo == 0 else None
                mm_n = C["p_g"].bump()
                cmm_last[qt] = mm_n

                def mk_mm(qt=qt, fo=fo, hb=hb, k=k, need_ev=need_ev,
                          ctx_d=ctx_d):
                    def f(t):
                        if fo == 0:
                            W(t, "d_in", ctx_d)
                        if k == 0:
                            W(t, "d_cast", wout_dma)
                        if need_ev is not None:
                            W(t, "v_ev", need_ev)
                        for fi in range(8):
                            i = t.matmul(
                                hb,
                                wout_bf[:, H * fi + 128 * fo:
                                        H * fi + 128 * fo + 128],
                                ctxs[qt % 2][:, 512 * fi:512 * fi + 512],
                                start=(fi == 0), stop=(fi == 7))
                        i.then_inc(C["p_g"].sem, 1)
                    return f
                pe_ops.append(mk_mm())
                hb_ev[k] = C["v_ev"].bump()

                def mk_ev(fo=fo, hb=hb, mm_n=mm_n, need_stage=need_stage,
                          sq=sq):
                    def f(v):
                        W(v, "p_g", mm_n)
                        if need_stage is not None:
                            W(v, "p_g", need_stage)
                        v.tensor_scalar_add(
                            stage8[sq][:, 512 * fo:512 * (fo + 1)], hb,
                            biasT[:, 24 + fo:25 + fo]).then_inc(
                            C["v_ev"].sem, 1)
                    return f
                dve_ops.append(mk_ev())
            stage_done_q[qt] = C["v_ev"].n

            for tt in range(4):
                ti = 4 * qt + tt
                par = ti % 2
                bank = CXb[ti % 2][:, 0:1024]
                need_cp = tile_cp.get(ti - 2)
                tr_n = C["p_g"].bump()
                if tt == 3:
                    tr_last[qt] = tr_n

                def mk_tr(tt=tt, bank=bank, sd=stage_done_q[qt],
                          need_cp=need_cp, sq=sq):
                    def f(t):
                        W(t, "v_ev", sd)
                        if need_cp is not None:
                            W(t, "v_ev", need_cp)
                        for fo in range(7):
                            t.transpose(bank[:, 128 * fo:128 * (fo + 1)],
                                        stage8[sq][:, 512 * fo + 128 * tt:
                                                   512 * fo + 128 * tt + 128],
                                        identB.ap())
                        t.transpose(bank[:, 896:1024],
                                    stage8[sq][:, 3584 + 128 * tt:
                                               3584 + 128 * tt + 128],
                                    identB.ap()).then_inc(C["p_g"].sem, 1)
                    return f
                pe_ops.append(mk_tr())
                tile_cp[ti] = C["v_ev"].bump()

                def mk_cp(bank=bank, tr_n=tr_n, par=par):
                    def f(v):
                        W(v, "p_g", tr_n)
                        v.tensor_copy(ptok[par].ap(), bank).then_inc(
                            C["v_ev"].sem, 1)
                    return f
                dve_ops.append(mk_cp())

                need_add = c_add.get(ti - 2)

                def mk_xload(ti=ti, par=par, need_add=need_add):
                    def f(sync):
                        if need_add is not None:
                            W(sync, "v_ap", need_add)
                        sync.dma_start(
                            xbuf[par].ap(),
                            d_x[128 * ti:128 * (ti + 1), :]).then_inc(
                            C["d_in"].sem, 16)
                    return f
                sync_ops.append(mk_xload())
                c_xd[ti] = C["d_in"].bump(16)

                c_sd2[ti] = ln_stats(ptok[par].ap(), par, dve_ops, act_ops,
                                     pre_waits=[("v_ev", tile_cp[ti])])
                if ti >= 1:
                    c_stage1(ti - 1)
                if ti >= 2:
                    c_stage2(ti - 2)
        c_stage1(31)
        c_stage2(30)
        c_stage2(31)

        with nc.Block() as block:
            @block.gpsimd
            def _(g):
                for f in gp_ops:
                    f(g)

            @block.tensor
            def _(t):
                for f in pe_ops:
                    f(t)

            @block.scalar
            def _(s):
                for f in act_ops:
                    f(s)

            @block.vector
            def _(v):
                for f in dve_ops:
                    f(v)

            @block.sync
            def _(sync):
                for f in sync_ops:
                    f(sync)

        c_es.close()
        w1buf = [sb(f"w1buf{i}", [128, 2 * H], BF16) for i in range(2)]
        w1T = [sb(f"w1T{i}", [128, 1024], BF16) for i in range(2)]
        w2buf = [sb(f"w2buf{i}", [128, DFF], BF16) for i in range(2)]
        w2T = [sb(f"w2T{i}", [128, DFF], BF16) for i in range(2)]
        hbu = sb("hbu", [128, 32 * 512], BF16)
        pe_ops, act_ops, dve_ops, sync_ops, gp_ops = [], [], [], [], []
        x2w_total = C["d_st"].n

        w1_tr, w2_tr = {}, {}
        w1_te, w2_te = {}, {}
        m1_h, m2_h, g_h, ye_h = {}, {}, {}, {}
        ytr_h, ycp_h = {}, {}
        d_add, d_ow = {}, {}
        w1_dma, w2_dma = {}, {}
        d_sd4, d_ap4 = {}, {}
        d_x2d = {}

        def emit_w1_load(e, j):
            if j >= 2:
                need_tr = w1_tr.get((e, j - 2))
            elif e > 0:
                need_tr = w1_tr.get((e - 1, j + 14))
            else:
                need_tr = None
            w1_dma[(e, j)] = C["d_cast"].bump(16)

            def mk_w1l(e=e, j=j, need_tr=need_tr):
                src2 = d_w1[e, 256 * j:256 * (j + 1), :].rearrange(
                    "(c p) d -> p c d", p=128)
                dst = w1buf[j % 2].ap().rearrange("p (c d) -> p c d", d=H)
                def f(g):
                    if need_tr is not None:
                        W(g, "p_g", need_tr)
                    g.dma_start(dst, src2).then_inc(C["d_cast"].sem, 16)
                return f
            gp_ops.append(mk_w1l())

        def emit_w2_load(e, dc):
            if dc >= 2:
                need_tr = w2_tr.get((e, dc - 2))
            elif e > 0:
                need_tr = w2_tr.get((e - 1, dc + 6))
            else:
                need_tr = None
            w2_dma[(e, dc)] = C["d_cast"].bump(16)

            def mk_w2l(e=e, dc=dc, need_tr=need_tr):
                src2 = d_w2[e, 128 * dc:128 * (dc + 1), :]
                def f(g):
                    if need_tr is not None:
                        W(g, "p_g", need_tr)
                    g.dma_start(w2buf[dc % 2].ap(), src2).then_inc(
                        C["d_cast"].sem, 16)
                return f
            gp_ops.append(mk_w2l())

        def emit_w1t(e, fc):
            j = fc // 2
            bank = CXb[fc % 2][:, 0:1024]
            te_need = w1_te.get((e, fc - 2)) if fc >= 2 else \
                (w1_te.get((e - 1, 30 + fc)) if e > 0 else None)
            w1_tr[(e, j)] = C["p_g"].bump()
            trn = w1_tr[(e, j)]

            def mk_w1t(fc=fc, j=j, bank=bank, te_need=te_need,
                       dma_n=w1_dma[(e, j)]):
                def f(t):
                    W(t, "d_cast", dma_n)
                    if te_need is not None:
                        W(t, "v_ev", te_need)
                    base = H * (fc % 2)
                    for c in range(7):
                        t.transpose(bank[:, 128 * c:128 * (c + 1)],
                                    w1buf[j % 2][:, base + 128 * c:
                                                 base + 128 * c + 128],
                                    identB.ap())
                    t.transpose(bank[:, 896:1024],
                                w1buf[j % 2][:, base + 896:base + 1024],
                                identB.ap()).then_inc(C["p_g"].sem, 1)
                return f
            pe_ops.append(mk_w1t())

            m1_need = m1_h.get((e, fc - 2)) if fc >= 2 else \
                (m1_h.get((e - 1, 30 + fc)) if e > 0 else None)
            w1_te[(e, fc)] = C["v_ev"].bump()

            def mk_w1e(fc=fc, bank=bank, trn=trn, m1_need=m1_need):
                def f(v):
                    W(v, "p_g", trn)
                    if m1_need is not None:
                        W(v, "p_g", m1_need)
                    v.tensor_copy(w1T[fc % 2].ap(), bank).then_inc(
                        C["v_ev"].sem, 1)
                return f
            dve_ops.append(mk_w1e())

        def emit_m1(e, fc):
            hb = HB[fc % 2]
            g_need = g_h.get((e, fc - 2)) if fc >= 2 else \
                (g_h.get((e - 1, 30 + fc)) if e > 0 else None)
            m2_guard = m2_h.get((e - 1, 7)) if (e > 0 and fc == 0) else None
            m1_h[(e, fc)] = C["p_g"].bump()
            mm_n = m1_h[(e, fc)]

            def mk_m1(e=e, fc=fc, hb=hb, te_n=w1_te[(e, fc)],
                      g_need=g_need):
                def f(t):
                    W(t, "v_ev", te_n)
                    if g_need is not None:
                        W(t, "a_g", g_need)
                    for dch in range(8):
                        i = t.matmul(
                            hb, w1T[fc % 2][:, 128 * dch:128 * dch + 128],
                            xln2T[:, S * dch + 512 * e:
                                  S * dch + 512 * e + 512],
                            start=(dch == 0), stop=(dch == 7))
                    i.then_inc(C["p_g"].sem, 1)
                return f
            pe_ops.append(mk_m1())

            g_h[(e, fc)] = C["a_g"].bump()

            def mk_gelu(e=e, fc=fc, hb=hb, mm_n=mm_n, m2_guard=m2_guard):
                def f(s):
                    W(s, "p_g", mm_n)
                    if m2_guard is not None:
                        W(s, "p_g", m2_guard)
                    s.activation(
                        hbu[:, 512 * fc:512 * (fc + 1)], hb, AF.Gelu,
                        bias=biasT[:, 96 + 32 * e + fc:
                                   97 + 32 * e + fc]).then_inc(
                        C["a_g"].sem, 1)
                return f
            act_ops.append(mk_gelu())

        def emit_w2t(e, dc):
            for b in range(4):
                bank = CXb[b % 2][:, 0:1024]
                if b >= 2:
                    te_need = w2_te.get((e, dc, b - 2))
                elif dc >= 1:
                    te_need = w2_te.get((e, dc - 1, b + 2))
                else:
                    te_need = w1_te.get((e, 30 + b))
                trn = C["p_g"].bump()
                if b == 3:
                    w2_tr[(e, dc)] = trn

                def mk_w2t(dc=dc, b=b, bank=bank, te_need=te_need,
                           dma_n=w2_dma[(e, dc)]):
                    def f(t):
                        W(t, "d_cast", dma_n)
                        if te_need is not None:
                            W(t, "v_ev", te_need)
                        for i2 in range(7):
                            fidx = 8 * b + i2
                            t.transpose(
                                bank[:, 128 * i2:128 * (i2 + 1)],
                                w2buf[dc % 2][:, 128 * fidx:
                                              128 * fidx + 128],
                                identB.ap())
                        fidx = 8 * b + 7
                        t.transpose(
                            bank[:, 896:1024],
                            w2buf[dc % 2][:, 128 * fidx:128 * fidx + 128],
                            identB.ap()).then_inc(C["p_g"].sem, 1)
                    return f
                pe_ops.append(mk_w2t())

                m2_need = m2_h.get((e, dc - 2)) if dc >= 2 else \
                    (m2_h.get((e - 1, dc + 6)) if e > 0 else None)
                w2_te[(e, dc, b)] = C["v_ev"].bump()

                def mk_w2e(dc=dc, b=b, bank=bank, trn=trn,
                           m2_need=m2_need):
                    def f(v):
                        W(v, "p_g", trn)
                        if b == 0 and m2_need is not None:
                            W(v, "p_g", m2_need)
                        v.tensor_copy(
                            w2T[dc % 2][:, 1024 * b:1024 * (b + 1)],
                            bank).then_inc(C["v_ev"].sem, 1)
                    return f
                dve_ops.append(mk_w2e())

        def emit_m2(e, dc):
            hb = HB[2 + dc % 2]
            ye_need = ye_h.get((e, dc - 2)) if dc >= 2 else \
                (ye_h.get((e - 1, dc + 6)) if e > 0 else None)
            m2_h[(e, dc)] = C["p_g"].bump()
            mm_n = m2_h[(e, dc)]

            def mk_m2(e=e, dc=dc, hb=hb, te_n=w2_te[(e, dc, 3)],
                      g_all=g_h[(e, 31)], ye_need=ye_need):
                def f(t):
                    W(t, "v_ev", te_n)
                    if dc == 0:
                        W(t, "a_g", g_all)
                    if ye_need is not None:
                        W(t, "v_ev", ye_need)
                    for fi in range(32):
                        i = t.matmul(
                            hb, w2T[dc % 2][:, 128 * fi:128 * fi + 128],
                            hbu[:, 512 * fi:512 * fi + 512],
                            start=(fi == 0), stop=(fi == 31))
                    i.then_inc(C["p_g"].sem, 1)
                return f
            pe_ops.append(mk_m2())

            ye_h[(e, dc)] = C["v_ev"].bump()
            ytr_need = ytr_h.get(e - 2) if (dc == 0 and e >= 2) else None

            def mk_ye(e=e, dc=dc, hb=hb, mm_n=mm_n, ytr_need=ytr_need):
                def f(v):
                    W(v, "p_g", mm_n)
                    if dc == 0 and ytr_need is not None:
                        W(v, "p_g", ytr_need)
                    v.tensor_scalar_add(
                        stage8[e % 2][:, 512 * dc:512 * (dc + 1)], hb,
                        biasT[:, 32 + 8 * e + dc:33 + 8 * e + dc]).then_inc(
                        C["v_ev"].sem, 1)
                return f
            dve_ops.append(mk_ye())

        def d_tail_stage1(ti):
            par = ti % 2
            d_ap4[ti] = ln_apply(ptok[par].ap(), ln_o[par].ap(), par,
                                 d_sd4[ti], dve_ops)
            need_ow = d_ow.get(ti - 2)
            d_add[ti] = C["v_ap"].bump()

            def mk_oadd(par=par, xd=d_x2d[ti], need_ow=need_ow):
                def f(v):
                    W(v, "d_in", xd)
                    if need_ow is not None:
                        W(v, "d_st", need_ow)
                    v.tensor_add(outt[par].ap(), x2t[par].ap(),
                                 ln_o[par].ap()).then_inc(C["v_ap"].sem, 1)
                return f
            dve_ops.append(mk_oadd())

            d_ow[ti] = C["d_st"].bump(16)

            def mk_ow(ti=ti, par=par, add_n=d_add[ti]):
                def f(sync):
                    W(sync, "v_ap", add_n)
                    sync.dma_start(d_out[128 * ti:128 * (ti + 1), :],
                                   outt[par].ap()).then_inc(
                        C["d_st"].sem, 16)
                return f
            sync_ops.append(mk_ow())

        def emit_ytile(e, tt):
            ti = 4 * e + tt
            par = ti % 2
            bank = CXb[ti % 2][:, 1024:2048]
            need_ycp = ycp_h.get(ti - 2)
            tr_n = C["p_g"].bump()
            if tt == 3:
                ytr_h[e] = tr_n

            def mk_ytr(tt=tt, bank=bank, sd=stage_done, need_ycp=need_ycp,
                       sq=e % 2):
                def f(t):
                    W(t, "v_ev", sd)
                    if need_ycp is not None:
                        W(t, "v_ev", need_ycp)
                    for dch in range(7):
                        t.transpose(bank[:, 128 * dch:128 * (dch + 1)],
                                    stage8[sq][:, 512 * dch + 128 * tt:
                                               512 * dch + 128 * tt + 128],
                                    identB.ap())
                    t.transpose(bank[:, 896:1024],
                                stage8[sq][:, 3584 + 128 * tt:
                                           3584 + 128 * tt + 128],
                                identB.ap()).then_inc(C["p_g"].sem, 1)
                return f
            pe_ops.append(mk_ytr())
            ycp_h[ti] = C["v_ev"].bump()

            def mk_ycp(bank=bank, tr_n=tr_n, par=par):
                def f(v):
                    W(v, "p_g", tr_n)
                    v.tensor_copy(ptok[par].ap(), bank).then_inc(
                        C["v_ev"].sem, 1)
                return f
            dve_ops.append(mk_ycp())

            need_add = d_add.get(ti - 2)

            def mk_x2l(ti=ti, par=par, need_add=need_add):
                def f(sync):
                    W(sync, "d_st", x2w_total)
                    if need_add is not None:
                        W(sync, "v_ap", need_add)
                    sync.dma_start(
                        x2t[par].ap(),
                        d_x2[128 * ti:128 * (ti + 1), :]).then_inc(
                        C["d_in"].sem, 16)
                return f
            sync_ops.append(mk_x2l())
            d_x2d[ti] = C["d_in"].bump(16)

            d_sd4[ti] = ln_stats(ptok[par].ap(), par, dve_ops, act_ops,
                                 pre_waits=[("v_ev", ycp_h[ti])])
            if ti >= 1:
                d_tail_stage1(ti - 1)

        for e in range(NEXP):
            emit_w1_load(e, 0)
            emit_w1t(e, 0)
            for fc in range(1, 32):
                if fc % 2 == 0:
                    emit_w1_load(e, fc // 2)
                emit_w1t(e, fc)
                emit_m1(e, fc - 1)
            emit_m1(e, 31)
            emit_w2_load(e, 0)
            emit_w2t(e, 0)
            for dc in range(1, 8):
                emit_w2_load(e, dc)
                emit_w2t(e, dc)
                emit_m2(e, dc - 1)
            emit_m2(e, 7)
            stage_done = C["v_ev"].n

            for tt in range(4):
                emit_ytile(e, tt)
        d_tail_stage1(31)

        final_st = C["d_st"].n

        with nc.Block() as block:
            @block.gpsimd
            def _(g):
                for f in gp_ops:
                    f(g)

            @block.tensor
            def _(t):
                for f in pe_ops:
                    f(t)

            @block.scalar
            def _(s):
                for f in act_ops:
                    f(s)

            @block.vector
            def _(v):
                for f in dve_ops:
                    f(v)

            @block.sync
            def _(sync):
                for f in sync_ops:
                    f(sync)
                sync.wait_ge(C["d_st"].sem, final_st)

    return nc


_NC = None


def kernel(L0_attn_x, L0_attn_w_qkv, L0_attn_b_qkv, L0_attn_w_out,
           L0_attn_b_out, L0_ffn_w1, L0_ffn_b1, L0_ffn_w2, L0_ffn_b2):
    global _NC
    if _NC is None:
        _NC = build()
    x = np.ascontiguousarray(np.asarray(L0_attn_x, dtype=np.float32))
    common = {
        "w_qkv": np.ascontiguousarray(L0_attn_w_qkv, dtype=np.float32),
        "b_qkv": np.ascontiguousarray(L0_attn_b_qkv, dtype=np.float32),
        "w_out": np.ascontiguousarray(L0_attn_w_out, dtype=np.float32),
        "b_out": np.ascontiguousarray(L0_attn_b_out, dtype=np.float32),
        "w1": np.ascontiguousarray(L0_ffn_w1, dtype=np.float32),
        "b1": np.ascontiguousarray(L0_ffn_b1, dtype=np.float32),
        "w2": np.ascontiguousarray(L0_ffn_w2, dtype=np.float32),
        "b2": np.ascontiguousarray(L0_ffn_b2, dtype=np.float32),
    }
    in_maps = [dict(common, x=np.ascontiguousarray(x[c])) for c in range(8)]
    res = run_bass_kernel_spmd(_NC, in_maps, core_ids=list(range(8)))
    return np.stack([np.asarray(r["out"], dtype=np.float32)
                     for r in res.results], axis=0)
